# version 05
# baseline (speedup 1.0000x reference)
import numpy as np
import ml_dtypes
import concourse.bass as bass
import concourse.mybir as mybir
from concourse.bass_utils import run_bass_kernel_spmd
from contextlib import ExitStack

F32 = mybir.dt.float32
BF16 = mybir.dt.bfloat16
AF = mybir.ActivationFunctionType
ALU = mybir.AluOpType
BF = ml_dtypes.bfloat16

DM, NJ = 4096, 32
INT, NU = 8192, 64
LAT, NL = 2048, 16
TC, TH, TQ = 1024, 512, 256
NCORES = 8
ISQ = float(1.0 / np.sqrt(128.0))
EPS = 1e-5
NEG = -1e30


class Sem:
    def __init__(self, h):
        self.h = h
        self.c = 0


class Prog:
    ENGS = ("tensor", "vector", "scalar", "gpsimd", "sync")

    def __init__(self):
        self.q = {k: [] for k in self.ENGS}
        self.trc = {k: [] for k in self.ENGS}
        self.enabled = True
        self.fence_v = None
        self.fence_s = None

    def op(self, eng, fn, inc=None, n=1):
        if not self.enabled:
            if inc is not None:
                inc.c += n
                return inc.c
            return None
        if inc is None:
            self.q[eng].append(fn)
            self.trc[eng].append(("op", None, 0))
            return None
        inc.c += n
        self.q[eng].append(lambda e, fn=fn, s=inc.h, n=n: fn(e).then_inc(s, n))
        self.trc[eng].append(("inc", id(inc), n))
        return inc.c

    def wait(self, eng, sem, val=None):
        if not self.enabled:
            return
        v = sem.c if val is None else val
        if v and v > 0:
            self.q[eng].append(lambda e, h=sem.h, v=v: e.wait_ge(h, v))
            self.trc[eng].append(("wait", id(sem), v))

    def dma(self, eng, out, in_, sem, **kw):
        return self.op(eng, lambda e, o=out, i=in_, kw=kw: e.dma_start(o, i, **kw),
                       inc=sem, n=16)

    def mm(self, out, lhsT, rhs, start, stop, inc=None, tile_position=None):
        return self.op("tensor",
                       lambda e, o=out, l=lhsT, r=rhs, s=start, st=stop,
                       tp=tile_position:
                       e.matmul(o, l, r, start=s, stop=st, tile_position=tp),
                       inc=inc)

    def tr(self, out, in_, ident, inc=None, tile_position=None):
        return self.op("tensor",
                       lambda e, o=out, i=in_, d=ident, tp=tile_position:
                       e.transpose(o, i, d, tile_position=tp), inc=inc)

    def act(self, out, in_, func, bias=0.0, scale=1.0, inc=None):
        return self.op("scalar",
                       lambda e, o=out, i=in_, f=func, b=bias, s=scale:
                       e.activation(o, i, f, bias=b, scale=s), inc=inc)

    def v_tt(self, out, a, b, alu, inc=None):
        return self.op("vector",
                       lambda e, o=out, x=a, y=b, al=alu: e.tensor_tensor(o, x, y, al),
                       inc=inc)

    def v_ts(self, out, a, s1, alu, inc=None, s2=None, alu2=None):
        if s2 is None:
            return self.op("vector",
                           lambda e, o=out, x=a, s=s1, al=alu:
                           e.tensor_scalar(o, x, s, None, al), inc=inc)
        return self.op("vector",
                       lambda e, o=out, x=a, s=s1, t=s2, al=alu, al2=alu2:
                       e.tensor_scalar(o, x, s, t, al, al2), inc=inc)

    def v_stt(self, out, in0, s1, in1, op0, op1, inc=None):
        return self.op("vector",
                       lambda e, o=out, x=in0, s=s1, y=in1, a=op0, b=op1:
                       e.scalar_tensor_tensor(o, x, s, y, a, b), inc=inc)

    def v_copy(self, out, in_, inc=None):
        return self.op("vector", lambda e, o=out, i=in_: e.tensor_copy(o, i), inc=inc)

    def v_red(self, out, in_, alu, inc=None):
        return self.op("vector",
                       lambda e, o=out, i=in_, al=alu:
                       e.tensor_reduce(o, i, mybir.AxisListType.X, al), inc=inc)

    def v_recip(self, out, in_, inc=None):
        return self.op("vector", lambda e, o=out, i=in_: e.reciprocal(o, i), inc=inc)

    def v_fence(self, sem, n=2):
        r = None
        for _ in range(n):
            self.wait("vector", sem)
            r = self.v_copy(self.fence_v[0], self.fence_v[1], inc=sem)
        return r

    def s_fence(self, sem, n=2):
        r = None
        for _ in range(n):
            self.wait("scalar", sem)
            r = self.act(self.fence_s[0], self.fence_s[1], AF.Identity, inc=sem)
        return r

    def replay(self, block):
        for name in self.ENGS:
            ops = self.q[name]

            def body(eng, ops=ops):
                for f in ops:
                    f(eng)
            getattr(block, name)(body)


def build_nc(debug=False, stop_after=None):
    nc = bass.Bass("TRN2", target_bir_lowering=False)
    P = Prog()
    dbg = dict(kind="ExternalOutput") if debug else {}

    def ckpt(name):
        if stop_after == name:
            P.enabled = False

    dI = lambda n, sh, dt=F32: nc.dram_tensor(n, sh, dt, kind="ExternalInput").ap()
    d_xT = dI("xT", [DM, TC])
    d_w = {"wq": dI("wq", [NJ, 128, DM]), "wk": dI("wk", [NJ, 128, DM]),
           "wv": dI("wv", [NJ, 128, DM]), "wo": dI("wo", [NJ, 128, DM]),
           "up": dI("upw", [NU, 128, DM]), "dn": dI("dnw", [NJ, 128, INT]),
           "f1": dI("f1w", [NL, 128, DM]), "f2": dI("f2w", [NJ, 128, LAT]),
           "ex": dI("exw", [NL, 128, LAT])}
    d_gate = dI("gatew", [128, NJ * 8])
    d_b = {k: dI(k, [128, NJ]) for k in ("bq", "bk", "bv", "bo", "gam", "bet")}
    d_id_bf = dI("id_bf", [128, 128], BF16)
    d_id_f = dI("id_f", [128, 128], F32)
    d_id8 = dI("id8", [8, 8])
    d_ones_bf = dI("ones_bf", [128, 128], BF16)
    d_ones_f = dI("ones_f", [128, 128])
    d_onec_bf = dI("onec_bf", [128, 1], BF16)
    d_diag = dI("diag", [128, 512])
    d_eps = dI("epsc", [128, 1])
    d_out = nc.dram_tensor("outT", [DM, TC], F32, kind="ExternalOutput").ap()

    s_hm_t = {k: nc.dram_tensor(f"s_{k}hm", [NJ, 128, TC], BF16, **dbg)
              for k in "qkv"}
    s_hm = {k: s_hm_t[k].ap() for k in "qkv"}
    s_z32 = nc.dram_tensor("s_z32", [NJ, 128, TC], F32, **dbg).ap()
    s_x232 = nc.dram_tensor("s_x232", [NJ, 128, TC], F32, **dbg).ap()
    s_x2T = nc.dram_tensor("s_x2T", [NJ, 128, TC], BF16, **dbg).ap()
    s_upB = nc.dram_tensor("s_upB", [NU, 128, TH], BF16, **dbg).ap()
    s_aout = (nc.dram_tensor("s_aout", [128, NJ * TC], BF16,
                             kind="ExternalOutput").ap() if debug else None)

    def hm_chunk_ap(k, a, qtr):
        off = (a * 8) * 128 * TC + qtr * TQ
        return bass.AP(s_hm_t[k], off, [[TC, 128], [128 * TC, 8], [1, TQ]])

    with ExitStack() as ctx:
        E = ctx.enter_context
        sW = Sem(E(nc.semaphore("sW")))
        sS = Sem(E(nc.semaphore("sS")))
        sP = Sem(E(nc.semaphore("sP")))
        sV = Sem(E(nc.semaphore("sV")))
        sA = Sem(E(nc.semaphore("sA")))

        c_id_bf = E(nc.sbuf_tensor("c_id_bf", [128, 128], BF16))
        c_id_f = E(nc.sbuf_tensor("c_id_f", [128, 128], F32))
        c_id8 = E(nc.sbuf_tensor("c_id8", [8, 8], F32))
        c_ones_bf = E(nc.sbuf_tensor("c_ones_bf", [128, 128], BF16))
        c_ones_f = E(nc.sbuf_tensor("c_ones_f", [128, 128], F32))
        c_onec = E(nc.sbuf_tensor("c_onec", [128, 1], BF16))
        c_diag = E(nc.sbuf_tensor("c_diag", [128, 512], F32))
        c_eps = E(nc.sbuf_tensor("c_eps", [128, 1], F32))
        c_gate = E(nc.sbuf_tensor("c_gate", [128, NJ * 8], BF16))
        c_b = {k: E(nc.sbuf_tensor("c_" + k, [128, NJ], F32))
               for k in ("bq", "bk", "bv", "bo", "gam", "bet")}
        fnsv = E(nc.sbuf_tensor("fnsv", [128, 16], F32))
        fnss = E(nc.sbuf_tensor("fnss", [128, 16], F32))
        P.fence_v = (fnsv[:, 0:8], fnsv[:, 8:16])
        P.fence_s = (fnss[:, 0:8], fnss[:, 8:16])

        wbuf = [E(nc.sbuf_tensor(f"wbuf{i}", [128, INT], BF16)) for i in range(2)]
        stf = [E(nc.sbuf_tensor(f"stf{i}", [128, TC], F32)) for i in range(4)]
        stb = [E(nc.sbuf_tensor(f"stb{i}", [128, TC], BF16)) for i in range(4)]
        mu_bc = E(nc.sbuf_tensor("mu_bc", [128, TC], F32))
        sq_bc = E(nc.sbuf_tensor("sq_bc", [128, TC], F32))
        r_bc = E(nc.sbuf_tensor("r_bc", [128, TC], F32))
        rw_bc = E(nc.sbuf_tensor("rw_bc", [128, TC], F32))

        blk = E(nc.Block())

        for dst, src in ((c_id_f, d_id_f), (c_id8, d_id8), (c_diag, d_diag),
                         (c_eps, d_eps), (c_ones_f, d_ones_f),
                         (c_b["bq"], d_b["bq"]), (c_b["bk"], d_b["bk"]),
                         (c_b["bv"], d_b["bv"]), (c_b["bo"], d_b["bo"]),
                         (c_b["gam"], d_b["gam"]), (c_b["bet"], d_b["bet"])):
            P.dma("sync", dst[:, :], src, sS)
        for dst, src in ((c_id_bf, d_id_bf), (c_ones_bf, d_ones_bf),
                         (c_onec, d_onec_bf), (c_gate, d_gate)):
            P.dma("gpsimd", dst[:, :], src, sW)
        P.wait("vector", sS)
        P.op("vector", lambda e: e.memset(fnsv[:, :], 1.0), inc=sV)
        P.op("scalar", lambda e: e.memzero(fnss[:, :]), inc=sA)
        P.wait("vector", sV)
        P.op("vector",
             lambda e: e.tensor_scalar_mul(c_b["bq"][:, :], c_b["bq"][:, :], ISQ),
             inc=sV)
        v_bq = P.v_fence(sV)

        with ExitStack() as sc:
            E2 = sc.enter_context
            xTbf = E2(nc.sbuf_tensor("xTbf", [128, NJ * TC], BF16))
            ps_qk = [E2(nc.psum_tensor(f"ps_qk{i}", [128, 512], F32)) for i in range(4)]

            for a in range(NJ):
                P.dma("gpsimd", xTbf[:, a * TC:(a + 1) * TC],
                      d_xT[a * 128:(a + 1) * 128, :], sW)
            w_xT = sW.c

            mm_done, ev_done, sp_done = {}, {}, {}
            for wi, (wkey, skey, bcol, bsc) in enumerate((
                    ("wq", "q", c_b["bq"], ISQ), ("wk", "k", c_b["bk"], 1.0),
                    ("wv", "v", c_b["bv"], 1.0))):
                for j in range(NJ):
                    t = wi * NJ + j
                    wbt = wbuf[t % 2]
                    if t >= 2:
                        P.wait("gpsimd", sP, mm_done[t - 2])
                    ld = P.dma("gpsimd", wbt[:, 0:DM], d_w[wkey][j, :, :], sW)
                    P.wait("tensor", sW, max(ld, w_xT))
                    if t >= 2:
                        P.wait("tensor", sA, ev_done[t - 2])
                    for half in range(2):
                        ps = ps_qk[2 * (t % 2) + half]
                        for m in range(NJ):
                            mm_done[t] = P.mm(
                                ps[:, :], wbt[:, m * 128:(m + 1) * 128],
                                xTbf[:, m * TC + half * 512:m * TC + (half + 1) * 512],
                                start=(m == 0), stop=(m == NJ - 1),
                                inc=(sP if (half == 1 and m == NJ - 1) else None))
                    P.wait("scalar", sP, mm_done[t])
                    if t == 0:
                        P.wait("scalar", sV, v_bq)
                    if t >= 2:
                        P.wait("scalar", sS, sp_done[t - 2])
                    for half in range(2):
                        ev_done[t] = P.act(
                            stb[t % 2][:, half * 512:(half + 1) * 512],
                            ps_qk[2 * (t % 2) + half][:, :], AF.Identity,
                            bias=bcol[:, j:j + 1], scale=bsc,
                            inc=(sA if half == 1 else None))
                    P.wait("sync", sA, ev_done[t])
                    sp_done[t] = P.dma("sync", s_hm[skey][j, :, :], stb[t % 2][:, :], sS)
            qkv_spilled = sS.c
            qkv_pe = sP.c
            ckpt("qkv")

        with ExitStack() as sc:
            E2 = sc.enter_context
            aout = E2(nc.sbuf_tensor("aout", [128, TC * 32], BF16))

            with ExitStack() as sc3:
                E3 = sc3.enter_context
                hm = {k: E3(nc.sbuf_tensor(f"hm_{k}", [128, 32 * TQ], BF16))
                      for k in "qkv"}
                hm3 = {k: hm[k].rearrange("p (g t) -> p t g", t=TQ) for k in "qkv"}
                efb = E3(nc.sbuf_tensor("efb", [128, 1024], F32))
                mkb = E3(nc.sbuf_tensor("mkb", [128, 1024], F32))
                esb = E3(nc.sbuf_tensor("esb", [128, 1024], BF16))
                vsb = E3(nc.sbuf_tensor("vsb", [128, 512], BF16))
                osb = E3(nc.sbuf_tensor("osb", [128, 512], BF16))
                rsb = E3(nc.sbuf_tensor("rsb", [128, 4], F32))
                ps_e = [E3(nc.psum_tensor(f"ps_e{i}", [128, 512], F32))
                        for i in range(2)]
                ps_v = [E3(nc.psum_tensor(f"ps_v{i}", [128, 128], BF16))
                        for i in range(2)]
                ps_od = [E3(nc.psum_tensor(f"ps_od{i}", [128, 512], F32))
                         for i in range(2)]
                ps_t = [E3(nc.psum_tensor(f"ps_t{i}", [128, 128], BF16))
                        for i in range(2)]

                P.wait("sync", sS, qkv_spilled)
                P.wait("vector", sP, qkv_pe)
                P.op("vector", lambda e: e.memset(ps_e[0][:, :], 0.0))
                P.op("vector", lambda e: e.memset(ps_e[1][:, :], 0.0), inc=sV)
                pe_zero = P.v_fence(sV)

                mm1_d, exp_d, mk_d, em_d, em_f = {}, {}, {}, {}, {}
                vT_d, ve_d, mm2_d, r_d, oe_d, oT_d, sc_d = {}, {}, {}, {}, {}, {}, {}

                def emit_tail(gg):
                    if gg < 0:
                        return
                    p2 = gg % 2
                    P.wait("tensor", sA, oe_d[gg])
                    if gg >= 2:
                        P.wait("tensor", sV, sc_d[gg - 2])
                    oT_d[gg] = P.tr(ps_t[p2][:, :],
                                    osb[:, (gg % 4) * 128:(gg % 4 + 1) * 128],
                                    c_id_bf[:, :], inc=sP)
                    P.wait("vector", sP, oT_d[gg])
                    sc_d[gg] = P.v_copy(aout[:, gg * 128:(gg + 1) * 128],
                                        ps_t[p2][:, :], inc=sV)

                for qtr in range(4):
                    for kk in "qkv":
                        for a in range(4):
                            if qtr > 0 and a == 0:
                                P.wait("sync", sP)
                            P.dma("sync", hm[kk][:, a * 8 * TQ:(a + 1) * 8 * TQ],
                                  hm_chunk_ap(kk, a, qtr), sS)
                    rl = sS.c
                    if qtr == 0:
                        ckpt("rl0")

                    for b in range(16):
                        gb = qtr * 16 + b
                        pe_b = ps_e[gb % 2]
                        ef = efb[:, (gb % 2) * 512:(gb % 2 + 1) * 512]
                        mk = mkb[:, (gb % 2) * 512:(gb % 2 + 1) * 512]
                        es = esb[:, (gb % 2) * 512:(gb % 2 + 1) * 512]
                        P.wait("tensor", sS, rl)
                        if gb == 0:
                            P.wait("tensor", sV, pe_zero)
                        if gb >= 2:
                            P.wait("tensor", sV, mk_d[gb - 2])
                            P.wait("tensor", sA, exp_d[gb - 2])
                        for i in range(16):
                            tq = b * 16 + i
                            tl = i % 4
                            mm1_d[gb] = P.mm(
                                pe_b[32 * tl:32 * (tl + 1), i * 32:(i + 1) * 32],
                                hm3["k"][:, tq:tq + 1, :], hm3["q"][:, tq:tq + 1, :],
                                start=True, stop=True, tile_position=(0, 32 * tl),
                                inc=(sP if i == 15 else None))
                        if gb == 0:
                            ckpt("mm1b0")
                        P.wait("scalar", sP, mm1_d[gb])
                        if gb >= 2:
                            P.wait("scalar", sV, em_d[gb - 2])
                        exp_d[gb] = P.act(ef, pe_b[:, :], AF.Exp, inc=sA)
                        P.wait("vector", sP, mm1_d[gb])
                        P.wait("vector", sA, exp_d[gb])
                        if gb >= 2:
                            P.wait("vector", sP, mm2_d[(gb - 2) * 4 + 3])
                        mk_d[gb] = P.v_stt(mk, pe_b[:, :], 0.001, c_diag[:, :],
                                           ALU.is_gt, ALU.mult, inc=sV)
                        P.wait("vector", sV, mk_d[gb])
                        em_d[gb] = P.v_tt(es, ef, mk, ALU.mult, inc=sV)
                        em_f[gb] = P.v_fence(sV, 1)
                        if gb == 0:
                            ckpt("bank0")

                        for g in range(4):
                            gg = gb * 4 + g
                            p2 = gg % 2
                            esg = esb[:, (gb % 2) * 512 + g * 128:
                                      (gb % 2) * 512 + (g + 1) * 128]
                            vs = vsb[:, (gg % 4) * 128:(gg % 4 + 1) * 128]
                            os_ = osb[:, (gg % 4) * 128:(gg % 4 + 1) * 128]
                            rc = rsb[:, gg % 4:gg % 4 + 1]
                            if gg >= 2:
                                P.wait("tensor", sV, ve_d[gg - 2])
                            for tl in range(4):
                                tq = b * 16 + g * 4 + tl
                                vT_d[gg] = P.tr(
                                    ps_v[p2][32 * tl:32 * (tl + 1), :],
                                    hm3["v"][:, tq:tq + 1, :], c_id_bf[:, :],
                                    tile_position=(0, 32 * tl),
                                    inc=(sP if tl == 3 else None))
                            if gg == 0:
                                ckpt("vt0")
                            P.wait("vector", sP, vT_d[gg])
                            if gg >= 4:
                                P.wait("vector", sP, mm2_d[gg - 4])
                            ve_d[gg] = P.v_copy(vs, ps_v[p2][:, :], inc=sV)
                            P.wait("tensor", sV, max(em_f[gb], ve_d[gg]))
                            if gg >= 2:
                                P.wait("tensor", sA, oe_d[gg - 2])
                                P.wait("tensor", sV, r_d[gg - 2])
                            P.mm(ps_od[p2][:, 0:128], esg, vs, start=True, stop=True)
                            mm2_d[gg] = P.mm(ps_od[p2][:, 128:129], esg, c_onec[:, :],
                                             start=True, stop=True, inc=sP)
                            P.wait("vector", sP, mm2_d[gg])
                            rt = P.v_ts(rc, ps_od[p2][:, 128:129], 1e-30, ALU.add,
                                        inc=sV)
                            P.wait("vector", sV, rt)
                            r_d[gg] = P.v_recip(rc, rc, inc=sV)
                            P.wait("scalar", sP, mm2_d[gg])
                            P.wait("scalar", sV, r_d[gg])
                            if gg >= 4:
                                P.wait("scalar", sP, oT_d[gg - 4])
                            oe_d[gg] = P.act(os_, ps_od[p2][:, 0:128], AF.Identity,
                                             scale=rc, inc=sA)
                            if gg == 0:
                                ckpt("grp0")
                            emit_tail(gg - 2)
                emit_tail(4 * 64 - 2)
                emit_tail(4 * 64 - 1)
                attn_v = P.v_fence(sV)
                if debug:
                    P.wait("sync", sV, attn_v)
                    P.dma("sync", s_aout, aout[:, :], sS)
                ckpt("attn")

            def aout_rhs(m, half):
                base = half * 512 * 32 + m
                return aout[:, base:base + 511 * 32 + 1:32]

            with ExitStack() as sc4:
                E4 = sc4.enter_context
                ps_po = [E4(nc.psum_tensor(f"ps_po{i}", [128, 512], F32))
                         for i in range(4)]
                ps_mu = [E4(nc.psum_tensor(f"ps_mu{i}", [128, 512], F32))
                         for i in range(2)]
                ps_sq = [E4(nc.psum_tensor(f"ps_sq{i}", [128, 512], F32))
                         for i in range(2)]

                mmo_d, z_d, zb_d, zq_d, spz_d, ldx_d, st_d = {}, {}, {}, {}, {}, {}, {}
                P.wait("tensor", sV, attn_v)
                P.wait("tensor", sA)

                def stats_j(j):
                    P.wait("tensor", sV, zq_d[j])
                    for half in range(2):
                        sl = slice(half * 512, (half + 1) * 512)
                        P.mm(ps_mu[half][:, :], c_ones_bf[:, :], stb[j % 2][:, sl],
                             start=(j == 0), stop=(j == NJ - 1))
                        st_d[j] = P.mm(ps_sq[half][:, :], c_ones_bf[:, :],
                                       stb[2 + j % 2][:, sl],
                                       start=(j == 0), stop=(j == NJ - 1),
                                       inc=(sP if half == 1 else None))

                P.wait("gpsimd", sP)
                for j in range(NJ):
                    if j >= 2:
                        P.wait("gpsimd", sP, mmo_d[j - 2])
                    ldw = P.dma("gpsimd", wbuf[j % 2][:, 0:DM], d_w["wo"][j, :, :], sW)
                    if j >= 2:
                        P.wait("sync", sV, z_d[j - 2])
                    ldx_d[j] = P.dma("sync", stf[2 + j % 2][:, :],
                                     d_xT[j * 128:(j + 1) * 128, :], sS)
                    P.wait("tensor", sW, ldw)
                    if j >= 2:
                        P.wait("tensor", sV, z_d[j - 2])
                    for half in range(2):
                        ps = ps_po[2 * (j % 2) + half]
                        for m in range(NJ):
                            mmo_d[j] = P.mm(
                                ps[:, :], wbuf[j % 2][:, m * 128:(m + 1) * 128],
                                aout_rhs(m, half),
                                start=(m == 0), stop=(m == NJ - 1),
                                inc=(sP if (half == 1 and m == NJ - 1) else None))
                    if j >= 1:
                        stats_j(j - 1)
                    P.wait("vector", sP, mmo_d[j])
                    P.wait("vector", sS, ldx_d[j])
                    if j >= 2:
                        P.wait("vector", sS, spz_d[j - 2])
                        P.wait("vector", sP, st_d[j - 2])
                    for half in range(2):
                        sl = slice(half * 512, (half + 1) * 512)
                        z_d[j] = P.v_stt(stf[j % 2][:, sl],
                                         ps_po[2 * (j % 2) + half][:, :],
                                         c_b["bo"][:, j:j + 1],
                                         stf[2 + j % 2][:, sl], ALU.add, ALU.add,
                                         inc=(sV if half == 1 else None))
                    P.wait("vector", sV, z_d[j])
                    zb_d[j] = P.v_copy(stb[j % 2][:, :], stf[j % 2][:, :], inc=sV)
                    P.wait("vector", sV, zb_d[j])
                    zq_d[j] = P.v_tt(stb[2 + j % 2][:, :], stb[j % 2][:, :],
                                     stb[j % 2][:, :], ALU.mult, inc=sV)
                    P.wait("sync", sV, z_d[j])
                    spz_d[j] = P.dma("sync", s_z32[j, :, :], stf[j % 2][:, :], sS)
                stats_j(NJ - 1)
                ln1_pe = sP.c

                P.wait("vector", sP, ln1_pe)
                for half in range(2):
                    sl = slice(half * 512, (half + 1) * 512)
                    P.v_ts(mu_bc[:, sl], ps_mu[half][:, :], 1.0 / DM, ALU.mult)
                    P.v_ts(sq_bc[:, sl], ps_sq[half][:, :], 1.0 / DM, ALU.mult,
                           inc=sV)
                P.wait("vector", sV)
                v1 = P.v_tt(r_bc[:, :], mu_bc[:, :], mu_bc[:, :], ALU.mult, inc=sV)
                P.wait("vector", sV, v1)
                P.v_tt(sq_bc[:, :], sq_bc[:, :], r_bc[:, :], ALU.subtract, inc=sV)
                P.v_fence(sV)
                P.wait("scalar", sV)
                P.act(sq_bc[:, :], sq_bc[:, :], AF.Sqrt, bias=c_eps[:, :], inc=sA)
                a_sd = P.s_fence(sA)
                P.wait("vector", sA, a_sd)
                ln1_r = P.v_recip(r_bc[:, :], sq_bc[:, :], inc=sV)
                ckpt("ln1")

        with ExitStack() as sc:
            E2 = sc.enter_context
            upT = E2(nc.sbuf_tensor("upT", [128, NU * 512], BF16))
            x2h = upT[:, 0:NJ * 512]
            latT = upT[:, NJ * 512:NJ * 512 + NL * 512]
            elatT = upT[:, NJ * 512 + NL * 512:NJ * 512 + 2 * NL * 512]

            with ExitStack() as sc5:
                E5 = sc5.enter_context
                x2Tr = E5(nc.sbuf_tensor("x2Tr", [128, NJ * TC], BF16))

                with ExitStack() as sc5b:
                    E5b = sc5b.enter_context
                    lgsb = rw_bc
                    lgt = E5b(nc.sbuf_tensor("lgt", [128, 64], F32))
                    m1sb = E5b(nc.sbuf_tensor("m1sb", [128, 8], F32))
                    m2sb = E5b(nc.sbuf_tensor("m2sb", [128, 8], F32))
                    eqsb = E5b(nc.sbuf_tensor("eqsb", [128, 8], F32))
                    xpsb = E5b(nc.sbuf_tensor("xpsb", [128, 8], F32))
                    rwsb = E5b(nc.sbuf_tensor("rwsb", [128, 8], F32))
                    rwsb2 = E5b(nc.sbuf_tensor("rwsb2", [128, 8], F32))
                    ps_g = [E5b(nc.psum_tensor(f"ps_g{i}", [8, 512], F32))
                            for i in range(2)]
                    ps_lt = E5b(nc.psum_tensor("ps_lt", [128, 64], F32))
                    ps_row = [E5b(nc.psum_tensor(f"ps_row{i}", [1, 512], F32))
                              for i in range(2)]
                    ps_rwb = [E5b(nc.psum_tensor(f"ps_rwb{i}", [128, 512], F32))
                              for i in range(2)]

                    rz_d, x2_d, x2b_d, spx_d = {}, {}, {}, {}
                    P.wait("vector", sV, ln1_r)
                    for j in range(NJ):
                        if j >= 2:
                            P.wait("sync", sV, x2b_d[j - 2])
                        rz_d[j] = P.dma("sync", stf[j % 2][:, :], s_z32[j, :, :], sS)
                        P.wait("vector", sS, rz_d[j])
                        if j >= 2:
                            P.wait("vector", sS, spx_d[j - 2])
                        t1 = P.v_tt(stf[2 + j % 2][:, :], stf[j % 2][:, :],
                                    mu_bc[:, :], ALU.subtract, inc=sV)
                        P.wait("vector", sV, t1)
                        x2_d[j] = P.v_tt(stf[2 + j % 2][:, :], stf[2 + j % 2][:, :],
                                         r_bc[:, :], ALU.mult, inc=sV)
                        P.wait("vector", sV, x2_d[j])
                        x2b_d[j] = P.v_copy(x2Tr[:, j * TC:(j + 1) * TC],
                                            stf[2 + j % 2][:, :], inc=sV)
                        P.wait("sync", sV, x2b_d[j])
                        P.dma("sync", s_x232[j, :, :], stf[2 + j % 2][:, :], sS)
                        spx_d[j] = P.dma("sync", s_x2T[j, :, :],
                                         x2Tr[:, j * TC:(j + 1) * TC], sS)
                    x2_fence = P.v_fence(sV)

                    g_d = None
                    P.wait("tensor", sV, x2_fence)
                    for j in range(NJ):
                        for half in range(2):
                            g_d = P.mm(
                                ps_g[half][:, :], c_gate[:, j * 8:(j + 1) * 8],
                                x2Tr[:, j * TC + half * 512:j * TC + (half + 1) * 512],
                                start=(j == 0), stop=(j == NJ - 1),
                                inc=(sP if (half == 1 and j == NJ - 1) else None))
                    P.wait("scalar", sP, g_d)
                    P.act(lgsb[0:8, 0:512], ps_g[0][:, :], AF.Identity)
                    P.act(lgsb[0:8, 512:1024], ps_g[1][:, :], AF.Identity, inc=sA)
                    a_lg = P.s_fence(sA)
                    P.wait("tensor", sA, a_lg)
                    t_lt = None
                    for s in range(8):
                        t_lt = P.tr(ps_lt[:, s * 8:(s + 1) * 8],
                                    lgsb[0:8, s * 128:(s + 1) * 128], c_id8[:, :],
                                    inc=(sP if s == 7 else None))
                    P.wait("vector", sP, t_lt)
                    v_lg = P.v_copy(lgt[:, :], ps_lt[:, :], inc=sV)
                    P.wait("vector", sV, v_lg)
                    P.v_red(m1sb[:, :], lgt.rearrange("p (s e) -> p s e", e=8),
                            ALU.max, inc=sV)
                    for s in range(8):
                        P.wait("vector", sV)
                        e1 = P.op("vector", lambda e, s=s: e.tensor_scalar(
                            eqsb[:, :], lgt[:, s * 8:(s + 1) * 8], m1sb[:, s:s + 1],
                            None, ALU.is_equal), inc=sV)
                        P.wait("vector", sV, e1)
                        e2 = P.v_stt(xpsb[:, :], eqsb[:, :], NEG,
                                     lgt[:, s * 8:(s + 1) * 8], ALU.mult, ALU.add,
                                     inc=sV)
                        P.wait("vector", sV, e2)
                        e3 = P.v_red(m2sb[:, s:s + 1], xpsb[:, :], ALU.max, inc=sV)
                        P.wait("vector", sV, e3)
                        P.v_tt(rwsb[:, s:s + 1], m1sb[:, s:s + 1], m2sb[:, s:s + 1],
                               ALU.add, inc=sV)
                    P.wait("vector", sV)
                    P.v_copy(rwsb2[:, :], rwsb[:, :], inc=sV)
                    P.op("vector", lambda e: e.memset(stf[0][:, :], 0.0), inc=sV)
                    v_ms = P.v_fence(sV)
                    P.wait("tensor", sV, v_ms)
                    t_rw = None
                    for s in range(8):
                        t_rw = P.tr(ps_row[s // 4][0:1, (s % 4) * 128:
                                                   (s % 4 + 1) * 128],
                                    rwsb2[:, s:s + 1], c_id_f[:, :],
                                    inc=(sP if s == 7 else None))
                    P.wait("vector", sP, t_rw)
                    P.v_copy(stf[0][0:1, 0:512], ps_row[0][0:1, :])
                    P.v_copy(stf[0][0:1, 512:1024], ps_row[1][0:1, :], inc=sV)
                    v_r0 = P.v_fence(sV)
                    P.wait("tensor", sV, v_r0)
                    t_rb = None
                    for half in range(2):
                        t_rb = P.mm(ps_rwb[half][:, :], c_ones_f[:, :],
                                    stf[0][:, half * 512:(half + 1) * 512],
                                    start=True, stop=True,
                                    inc=(sP if half == 1 else None))
                    P.wait("vector", sP, t_rb)
                    P.v_copy(rw_bc[:, 0:512], ps_rwb[0][:, :])
                    P.v_copy(rw_bc[:, 512:1024], ps_rwb[1][:, :], inc=sV)
                    ckpt("gate")

                squ_d = {}
                up_fence = None
                with ExitStack() as sc6:
                    E6 = sc6.enter_context
                    ps_up = [E6(nc.psum_tensor(f"ps_up{i}", [128, 512], F32))
                             for i in range(4)]
                    mmu_d, rl_d, spu_d = {}, {}, {}
                    P.wait("gpsimd", sP)
                    P.wait("tensor", sV)
                    P.wait("tensor", sA)
                    for j in range(NU):
                        if j >= 2:
                            P.wait("gpsimd", sP, mmu_d[j - 2])
                        ldw = P.dma("gpsimd", wbuf[j % 2][:, 0:DM],
                                    d_w["up"][j, :, :], sW)
                        P.wait("tensor", sW, ldw)
                        if j >= 2:
                            P.wait("tensor", sV, squ_d[j - 2])
                        for half in range(2):
                            ps = ps_up[2 * (j % 2) + half]
                            for m in range(NJ):
                                mmu_d[j] = P.mm(
                                    ps[:, :], wbuf[j % 2][:, m * 128:(m + 1) * 128],
                                    x2Tr[:, m * TC + half * 512:
                                         m * TC + (half + 1) * 512],
                                    start=(m == 0), stop=(m == NJ - 1),
                                    inc=(sP if (half == 1 and m == NJ - 1) else None))
                        P.wait("vector", sP, mmu_d[j])
                        if j >= 2:
                            P.wait("vector", sS, spu_d[j - 2])
                        for half in range(2):
                            sl = slice(half * 512, (half + 1) * 512)
                            rl_d[j] = P.v_ts(stf[j % 2][:, sl],
                                             ps_up[2 * (j % 2) + half][:, :],
                                             0.0, ALU.max,
                                             inc=(sV if half == 1 else None))
                        P.wait("vector", sV, rl_d[j])
                        relu = stf[j % 2]
                        P.v_tt(upT[:, j * 512:(j + 1) * 512], relu[:, 0:512],
                               relu[:, 0:512], ALU.mult)
                        squ_d[j] = P.v_tt(stb[j % 2][:, 0:512], relu[:, 512:1024],
                                          relu[:, 512:1024], ALU.mult, inc=sV)
                        P.wait("sync", sV, squ_d[j])
                        spu_d[j] = P.dma("sync", s_upB[j, :, :], stb[j % 2][:, 0:512],
                                         sS)
                    up_fence = P.v_fence(sV)
                up_spilled = sS.c
                ckpt("up")

            def half_pipeline(h):
                hoff = h * 512
                with ExitStack() as sch:
                    Eh = sch.enter_context
                    shr = Eh(nc.sbuf_tensor(f"shr{h}", [128, NJ * 512], F32))
                    ps_r = [Eh(nc.psum_tensor(f"ps_r{h}{i}", [128, 512], F32))
                            for i in range(2)]
                    ps_sa = [Eh(nc.psum_tensor(f"ps_sa{h}{i}", [128, 512], F32))
                             for i in range(2)]
                    ps_s2 = [Eh(nc.psum_tensor(f"ps_s2{h}{i}", [128, 512], F32))
                             for i in range(2)]

                    P.wait("tensor", sV)
                    P.wait("tensor", sA)
                    if h == 0:
                        P.wait("tensor", sV, up_fence)
                    else:
                        P.wait("sync", sS, up_spilled)
                        P.wait("sync", sP)
                        rlu = None
                        for a in range(NU):
                            rlu = P.dma("sync", upT[:, a * 512:(a + 1) * 512],
                                        s_upB[a, :, :], sS)
                        P.wait("tensor", sS, rlu)
                    mmd_d, evd_d = {}, {}
                    P.wait("gpsimd", sP)
                    for j2 in range(NJ):
                        if j2 >= 2:
                            P.wait("gpsimd", sP, mmd_d[j2 - 2])
                        ldw = P.dma("gpsimd", wbuf[j2 % 2][:, 0:INT],
                                    d_w["dn"][j2, :, :], sW)
                        P.wait("tensor", sW, ldw)
                        if j2 >= 2:
                            P.wait("tensor", sV, evd_d[j2 - 2])
                        ps = ps_r[j2 % 2]
                        for m in range(NU):
                            mmd_d[j2] = P.mm(
                                ps[:, :], wbuf[j2 % 2][:, m * 128:(m + 1) * 128],
                                upT[:, m * 512:(m + 1) * 512],
                                start=(m == 0), stop=(m == NU - 1),
                                inc=(sP if m == NU - 1 else None))
                        P.wait("vector", sP, mmd_d[j2])
                        evd_d[j2] = P.v_copy(shr[:, j2 * 512:(j2 + 1) * 512],
                                             ps[:, :], inc=sV)

                    P.wait("sync", sP, mmd_d[NJ - 1])
                    rlx = None
                    for j in range(NJ):
                        rlx = P.dma("sync", x2h[:, j * 512:(j + 1) * 512],
                                    s_x2T[j, :, hoff:hoff + 512], sS)
                    P.wait("tensor", sS, rlx)

                    mmf_d, evf_d = {}, {}
                    P.wait("gpsimd", sP)
                    for j3 in range(NL):
                        if j3 >= 2:
                            P.wait("gpsimd", sP, mmf_d[j3 - 2])
                        ldw = P.dma("gpsimd", wbuf[j3 % 2][:, 0:DM],
                                    d_w["f1"][j3, :, :], sW)
                        P.wait("tensor", sW, ldw)
                        if j3 >= 2:
                            P.wait("tensor", sV, evf_d[j3 - 2])
                        ps = ps_r[j3 % 2]
                        for m in range(NJ):
                            mmf_d[j3] = P.mm(
                                ps[:, :], wbuf[j3 % 2][:, m * 128:(m + 1) * 128],
                                x2h[:, m * 512:(m + 1) * 512],
                                start=(m == 0), stop=(m == NJ - 1),
                                inc=(sP if m == NJ - 1 else None))
                        P.wait("vector", sP, mmf_d[j3])
                        evf_d[j3] = P.v_copy(latT[:, j3 * 512:(j3 + 1) * 512],
                                             ps[:, :], inc=sV)
                    f1_fence = P.v_fence(sV)

                    mme_d, eve_d = {}, {}
                    P.wait("gpsimd", sP)
                    for j4 in range(NL):
                        if j4 >= 2:
                            P.wait("gpsimd", sP, mme_d[j4 - 2])
                        ldw = P.dma("gpsimd", wbuf[j4 % 2][:, 0:LAT],
                                    d_w["ex"][j4, :, :], sW)
                        P.wait("tensor", sW, ldw)
                        P.wait("tensor", sV, f1_fence)
                        if j4 >= 2:
                            P.wait("tensor", sV, eve_d[j4 - 2])
                        ps = ps_r[j4 % 2]
                        for m in range(NL):
                            mme_d[j4] = P.mm(
                                ps[:, :], wbuf[j4 % 2][:, m * 128:(m + 1) * 128],
                                latT[:, m * 512:(m + 1) * 512],
                                start=(m == 0), stop=(m == NL - 1),
                                inc=(sP if m == NL - 1 else None))
                        P.wait("vector", sP, mme_d[j4])
                        eve_d[j4] = P.v_copy(elatT[:, j4 * 512:(j4 + 1) * 512],
                                             ps[:, :], inc=sV)
                    ex_fence = P.v_fence(sV)

                    mm2_dd, cmb_d, fq_d, sf_d = {}, {}, {}, {}

                    def stats_aff(j5):
                        P.wait("tensor", sV, fq_d[j5])
                        P.mm(ps_sa[0][:, :], c_ones_bf[:, :], stb[j5 % 2][:, 0:512],
                             start=(j5 == 0), stop=(j5 == NJ - 1))
                        sf_d[j5] = P.mm(ps_sa[1][:, :], c_ones_bf[:, :],
                                        stb[2 + j5 % 2][:, 0:512],
                                        start=(j5 == 0), stop=(j5 == NJ - 1), inc=sP)

                    P.wait("gpsimd", sP)
                    for j5 in range(NJ):
                        if j5 >= 2:
                            P.wait("gpsimd", sP, mm2_dd[j5 - 2])
                        ldw = P.dma("gpsimd", wbuf[j5 % 2][:, 0:LAT],
                                    d_w["f2"][j5, :, :], sW)
                        P.wait("tensor", sW, ldw)
                        P.wait("tensor", sV, ex_fence)
                        if j5 >= 2:
                            P.wait("tensor", sV, cmb_d[j5 - 2])
                        ps = ps_r[j5 % 2]
                        for m in range(NL):
                            mm2_dd[j5] = P.mm(
                                ps[:, :], wbuf[j5 % 2][:, m * 128:(m + 1) * 128],
                                elatT[:, m * 512:(m + 1) * 512],
                                start=(m == 0), stop=(m == NL - 1),
                                inc=(sP if m == NL - 1 else None))
                        if j5 >= 1:
                            stats_aff(j5 - 1)
                        P.wait("vector", sP, mm2_dd[j5])
                        if j5 >= 2:
                            P.wait("vector", sP, sf_d[j5 - 2])
                        c1 = P.v_tt(stf[j5 % 2][:, 0:512], ps[:, :],
                                    rw_bc[:, hoff:hoff + 512], ALU.mult, inc=sV)
                        P.wait("vector", sV, c1)
                        cmb_d[j5] = P.v_tt(shr[:, j5 * 512:(j5 + 1) * 512],
                                           stf[j5 % 2][:, 0:512],
                                           shr[:, j5 * 512:(j5 + 1) * 512], ALU.add,
                                           inc=sV)
                        P.wait("vector", sV, cmb_d[j5])
                        fb = P.v_copy(stb[j5 % 2][:, 0:512],
                                      shr[:, j5 * 512:(j5 + 1) * 512], inc=sV)
                        P.wait("vector", sV, fb)
                        fq_d[j5] = P.v_tt(stb[2 + j5 % 2][:, 0:512],
                                          stb[j5 % 2][:, 0:512],
                                          stb[j5 % 2][:, 0:512], ALU.mult, inc=sV)
                    stats_aff(NJ - 1)

                    def ln_finalize(ps_mu_, ps_sq_, sl):
                        P.wait("vector", sP)
                        P.v_ts(mu_bc[:, sl], ps_mu_[:, :], 1.0 / DM, ALU.mult)
                        P.v_ts(sq_bc[:, sl], ps_sq_[:, :], 1.0 / DM, ALU.mult,
                               inc=sV)
                        P.wait("vector", sV)
                        v1 = P.v_tt(r_bc[:, sl], mu_bc[:, sl], mu_bc[:, sl],
                                    ALU.mult, inc=sV)
                        P.wait("vector", sV, v1)
                        P.v_tt(sq_bc[:, sl], sq_bc[:, sl], r_bc[:, sl], ALU.subtract,
                               inc=sV)
                        P.v_fence(sV)
                        P.wait("scalar", sV)
                        P.act(sq_bc[:, sl], sq_bc[:, sl], AF.Sqrt, bias=c_eps[:, :],
                              inc=sA)
                        fs = P.s_fence(sA)
                        P.wait("vector", sA, fs)
                        return P.v_recip(r_bc[:, sl], sq_bc[:, sl], inc=sV)

                    vr = ln_finalize(ps_sa[0], ps_sa[1], slice(0, 512))

                    gq_d, s2_d, rx_d = {}, {}, {}

                    def stats_2(j5):
                        P.wait("tensor", sV, gq_d[j5])
                        P.mm(ps_s2[0][:, :], c_ones_bf[:, :], stb[j5 % 2][:, 512:1024],
                             start=(j5 == 0), stop=(j5 == NJ - 1))
                        s2_d[j5] = P.mm(ps_s2[1][:, :], c_ones_bf[:, :],
                                        stb[2 + j5 % 2][:, 512:1024],
                                        start=(j5 == 0), stop=(j5 == NJ - 1), inc=sP)

                    P.wait("vector", sV, vr)
                    for j5 in range(NJ):
                        if j5 >= 2:
                            P.wait("sync", sV, gq_d[j5 - 2])
                        rx_d[j5] = P.dma("sync", stf[2 + j5 % 2][:, 0:512],
                                         s_x232[j5, :, hoff:hoff + 512], sS)
                        if j5 >= 2:
                            P.wait("vector", sP, s2_d[j5 - 2])
                        y1 = P.v_tt(stf[j5 % 2][:, 512:1024],
                                    shr[:, j5 * 512:(j5 + 1) * 512], mu_bc[:, 0:512],
                                    ALU.subtract, inc=sV)
                        P.wait("vector", sV, y1)
                        y2 = P.v_tt(stf[j5 % 2][:, 512:1024],
                                    stf[j5 % 2][:, 512:1024], r_bc[:, 0:512],
                                    ALU.mult, inc=sV)
                        P.wait("vector", sV, y2)
                        y3 = P.v_ts(stf[2 + j5 % 2][:, 512:1024],
                                    stf[j5 % 2][:, 512:1024],
                                    c_b["gam"][:, j5:j5 + 1], ALU.mult,
                                    s2=c_b["bet"][:, j5:j5 + 1], alu2=ALU.add,
                                    inc=sV)
                        P.wait("vector", sV, y3)
                        P.wait("vector", sS, rx_d[j5])
                        g1 = P.v_tt(shr[:, j5 * 512:(j5 + 1) * 512],
                                    stf[2 + j5 % 2][:, 512:1024],
                                    stf[2 + j5 % 2][:, 0:512], ALU.add, inc=sV)
                        P.wait("vector", sV, g1)
                        gb = P.v_copy(stb[j5 % 2][:, 512:1024],
                                      shr[:, j5 * 512:(j5 + 1) * 512], inc=sV)
                        P.wait("vector", sV, gb)
                        gq_d[j5] = P.v_tt(stb[2 + j5 % 2][:, 512:1024],
                                          stb[j5 % 2][:, 512:1024],
                                          stb[j5 % 2][:, 512:1024], ALU.mult, inc=sV)
                        if j5 >= 1:
                            stats_2(j5 - 1)
                    stats_2(NJ - 1)

                    vr2 = ln_finalize(ps_s2[0], ps_s2[1], slice(512, 1024))

                    P.wait("vector", sV, vr2)
                    o_d = {}
                    for j5 in range(NJ):
                        if j5 >= 2:
                            P.wait("vector", sS, o_d[j5 - 2])
                        o1 = P.v_tt(stf[j5 % 2][:, 0:512],
                                    shr[:, j5 * 512:(j5 + 1) * 512],
                                    mu_bc[:, 512:1024], ALU.subtract, inc=sV)
                        P.wait("vector", sV, o1)
                        o2 = P.v_tt(stf[j5 % 2][:, 0:512], stf[j5 % 2][:, 0:512],
                                    r_bc[:, 512:1024], ALU.mult, inc=sV)
                        P.wait("sync", sV, o2)
                        o_d[j5] = P.dma("sync",
                                        d_out[j5 * 128:(j5 + 1) * 128,
                                              hoff:hoff + 512],
                                        stf[j5 % 2][:, 0:512], sS)
                    return sS.c

            half_pipeline(0)
            ckpt("half0")
            out_done = half_pipeline(1)

        P.wait("sync", sS, out_done)
        P.replay(blk)
    build_nc.last_prog = P
    return nc


def _tiles_T(W):
    o, i = W.shape
    nj, ni = o // 128, i // 128
    return np.ascontiguousarray(
        W.reshape(nj, 128, ni, 128).transpose(0, 3, 2, 1).reshape(nj, 128, i))


def _tiles_N(W):
    i, o = W.shape
    ni, nj = i // 128, o // 128
    return np.ascontiguousarray(
        W.reshape(ni, 128, nj, 128).transpose(2, 1, 0, 3).reshape(nj, 128, i))


def _cols(v):
    return np.ascontiguousarray(v.reshape(-1, 128).T)


def kernel(L0_attn_x, L0_attn_Wq, L0_attn_bq, L0_attn_Wk, L0_attn_bk,
           L0_attn_Wv, L0_attn_bv, L0_attn_Wo, L0_attn_bo,
           L0_ffn_gate_w, L0_ffn_up_w, L0_ffn_down_w, L0_ffn_fc1_w,
           L0_ffn_fc2_w, L0_ffn_expert_w, L0_ffn_ln_gamma, L0_ffn_ln_beta):
    x = np.asarray(L0_attn_x, dtype=np.float32).reshape(8192, DM)
    xT = np.ascontiguousarray(x.T)

    shared = {
        "wq": _tiles_T(np.asarray(L0_attn_Wq, np.float32)),
        "wk": _tiles_T(np.asarray(L0_attn_Wk, np.float32)),
        "wv": _tiles_T(np.asarray(L0_attn_Wv, np.float32)),
        "wo": _tiles_T(np.asarray(L0_attn_Wo, np.float32)),
        "upw": _tiles_N(np.asarray(L0_ffn_up_w, np.float32)),
        "dnw": _tiles_N(np.asarray(L0_ffn_down_w, np.float32)),
        "f1w": _tiles_N(np.asarray(L0_ffn_fc1_w, np.float32)),
        "f2w": _tiles_N(np.asarray(L0_ffn_fc2_w, np.float32)),
        "exw": _tiles_N(np.asarray(L0_ffn_expert_w, np.float32)),
        "gatew": np.ascontiguousarray(
            np.asarray(L0_ffn_gate_w, np.float32).reshape(NJ, 128, 8)
            .transpose(1, 0, 2).reshape(128, NJ * 8)),
        "bq": _cols(np.asarray(L0_attn_bq, np.float32)),
        "bk": _cols(np.asarray(L0_attn_bk, np.float32)),
        "bv": _cols(np.asarray(L0_attn_bv, np.float32)),
        "bo": _cols(np.asarray(L0_attn_bo, np.float32)),
        "gam": _cols(np.asarray(L0_ffn_ln_gamma, np.float32)),
        "bet": _cols(np.asarray(L0_ffn_ln_beta, np.float32)),
        "id_bf": np.eye(128, dtype=np.float32).astype(BF),
        "id_f": np.eye(128, dtype=np.float32),
        "id8": np.eye(8, dtype=np.float32),
        "ones_bf": np.ones((128, 128), np.float32).astype(BF),
        "ones_f": np.ones((128, 128), np.float32),
        "onec_bf": np.ones((128, 1), np.float32).astype(BF),
        "epsc": np.full((128, 1), EPS, np.float32),
    }
    diag = np.zeros((128, 512), np.float32)
    for tok in range(16):
        tl = tok % 4
        diag[tl * 32:(tl + 1) * 32, tok * 32:(tok + 1) * 32] = 1.0
    shared["diag"] = diag

    in_maps = []
    for c in range(NCORES):
        m = dict(shared)
        m["xT"] = np.ascontiguousarray(xT[:, c * TC:(c + 1) * TC])
        in_maps.append(m)

    nc = build_nc()
    res = run_bass_kernel_spmd(nc, in_maps, core_ids=list(range(NCORES)))
    outT = np.concatenate([np.asarray(res.results[c]["outT"])
                           for c in range(NCORES)], axis=1)
    return np.ascontiguousarray(outT.T).reshape(1, 8192, DM).astype(np.float32)
